# version 01
# baseline (speedup 1.0000x reference)
import numpy as np
import ml_dtypes
import concourse.bass as bass
import concourse.mybir as mybir
from concourse.bass_utils import run_bass_kernel_spmd

F32 = mybir.dt.float32
BF16 = mybir.dt.bfloat16
I32 = mybir.dt.int32
AF = mybir.ActivationFunctionType
ALU = mybir.AluOpType
AX = mybir.AxisListType

D = 1024
NH = 16
HD = 64
EPS_LN = 1e-5
EPS_RMS = 1.1920929e-07
L0_EXPB = -10.0
L1_EXPB = -20.0
NEG = -1e30

SS_ROWS = 5120
SQ0, SQ1 = 8, 40
RX_ROWS = 3328
RQ = 22
RK = 26
OWN0 = 3
NOWN = 16
NCHUNK = 10


class Sem:
    def __init__(self, nc, name):
        self.h = nc.alloc_semaphore(name)
        self.n = 0

    def inc(self, ins, k=1):
        ins.then_inc(self.h, k)
        self.n += k
        return self.n

    def dinc(self, ins):
        return self.inc(ins, 16)


class EngW:

    def __init__(self, eng, sem):
        self._e = eng
        self._s = sem
        self._last = None

    def __getattr__(self, name):
        f = getattr(self._e, name)
        if not callable(f):
            return f

        def call(*a, **kw):
            r = f(*a, **kw)
            if hasattr(r, "then_inc"):
                self._last = r
            return r
        return call

    def fence(self):
        if self._last is not None:
            m = self._s.inc(self._last)
            self._e.wait_ge(self._s.h, m)
            self._last = None
        return self._s.n

    def mark(self):
        if self._last is not None:
            m = self._s.inc(self._last)
            self._last = None
            return m
        return self._s.n


def build(taps=(), tap_hp=0):
    taps = set(taps)
    nc = bass.Bass("TRN2", target_bir_lowering=False)
    TE, VE, AE, SY, GP = nc.tensor, nc.vector, nc.scalar, nc.sync, nc.gpsimd

    sD = Sem(nc, "sD")
    sG = Sem(nc, "sG")
    sC = Sem(nc, "sC")
    sT = Sem(nc, "sT")
    sV = Sem(nc, "sV")
    sA = Sem(nc, "sA")

    WNAMES = [
        "L0_attn_w_q", "L0_attn_b_q", "L0_attn_w_k", "L0_attn_b_k",
        "L0_attn_w_v", "L0_attn_b_v", "L0_attn_rms_q_w", "L0_attn_rms_k_w",
        "L0_attn_w_o", "L0_attn_b_o",
        "L0_ffn_gate_w", "L0_ffn_gate_b", "L0_ffn_w1", "L0_ffn_b1",
        "L0_ffn_w2", "L0_ffn_b2",
        "L1_attn_ln1_g", "L1_attn_ln1_b", "L1_attn_wq", "L1_attn_bq",
        "L1_attn_wk", "L1_attn_bk", "L1_attn_wv", "L1_attn_bv",
        "L1_attn_wproj", "L1_attn_bproj", "L1_attn_ln2_g", "L1_attn_ln2_b",
        "L1_attn_w1", "L1_attn_b1", "L1_attn_w2", "L1_attn_b2",
        "L1_ffn_gate_w", "L1_ffn_gate_b", "L1_ffn_expert_bias",
        "L1_ffn_w1", "L1_ffn_b1", "L1_ffn_w2", "L1_ffn_b2",
    ]
    WSHAPES = {
        "L0_attn_w_q": [D, D], "L0_attn_b_q": [1, D],
        "L0_attn_w_k": [D, D], "L0_attn_b_k": [1, D],
        "L0_attn_w_v": [D, 256], "L0_attn_b_v": [1, 256],
        "L0_attn_rms_q_w": [1, 64], "L0_attn_rms_k_w": [1, 64],
        "L0_attn_w_o": [D, D], "L0_attn_b_o": [1, D],
        "L0_ffn_gate_w": [D, 8], "L0_ffn_gate_b": [1, 8],
        "L0_ffn_w1": [8, D, 4096], "L0_ffn_b1": [8, 128, 32],
        "L0_ffn_w2": [8, 4096, D], "L0_ffn_b2": [8, 1, D],
        "L1_attn_ln1_g": [1, D], "L1_attn_ln1_b": [1, D],
        "L1_attn_wq": [D, D], "L1_attn_bq": [1, D],
        "L1_attn_wk": [D, D], "L1_attn_bk": [1, D],
        "L1_attn_wv": [D, D], "L1_attn_bv": [1, D],
        "L1_attn_wproj": [D, D], "L1_attn_bproj": [1, D],
        "L1_attn_ln2_g": [1, D], "L1_attn_ln2_b": [1, D],
        "L1_attn_w1": [D, 4096], "L1_attn_b1": [128, 32],
        "L1_attn_w2": [4096, D], "L1_attn_b2": [1, D],
        "L1_ffn_gate_w": [8, D], "L1_ffn_gate_b": [1, 8],
        "L1_ffn_expert_bias": [1, 8],
        "L1_ffn_w1": [8, D, 4096], "L1_ffn_b1": [8, 128, 32],
        "L1_ffn_w2": [8, 4096, D], "L1_ffn_b2": [8, 1, D],
    }
    IN = {n: nc.dram_tensor(n, WSHAPES[n], F32, kind="ExternalInput")
          for n in WNAMES}
    xstat_d = nc.dram_tensor("xstat", [SS_ROWS, D], F32, kind="ExternalInput")
    xin_d = nc.dram_tensor("xin", [RX_ROWS, D], F32, kind="ExternalInput")
    l0eb_d = nc.dram_tensor("l0eb", [128, RK], F32, kind="ExternalInput")
    l1eb_d = nc.dram_tensor("l1eb", [128, RQ], F32, kind="ExternalInput")
    gam_d = nc.dram_tensor("gam", [1, NCHUNK * 512], F32, kind="ExternalInput")
    dynoff_d = nc.dram_tensor("dynoff", [1, 2], I32, kind="ExternalInput")
    l1bqc_d = nc.dram_tensor("l1bqc", [128, 8], F32, kind="ExternalInput")
    l1bkc_d = nc.dram_tensor("l1bkc", [128, 8], F32, kind="ExternalInput")
    yout_d = nc.dram_tensor("yout", [2048, D], F32, kind="ExternalOutput")

    invd_d = nc.dram_tensor("invd_s", [16, 128, 40], F32, kind="Internal")
    xnrel_d = nc.dram_tensor("xnrel_s", [RX_ROWS, D], F32, kind="Internal")
    x1_d = nc.dram_tensor("x1_s", [RQ * 128, D], F32, kind="Internal")
    s0_d = nc.dram_tensor("s0_s", [RQ * 128, D], F32, kind="Internal")
    xn1_d = nc.dram_tensor("xn1_s", [RQ * 128, D], F32, kind="Internal")
    out2_d = nc.dram_tensor("out2_s", [NOWN * 128, D], F32, kind="Internal")
    s1_d = nc.dram_tensor("s1_s", [NOWN * 128, D], F32, kind="Internal")
    w1l0_bf = nc.dram_tensor("w1l0_bf", [8, 32, 128, 1024], BF16, kind="Internal")
    w2l0_bf = nc.dram_tensor("w2l0_bf", [8, 32, 128, 1024], BF16, kind="Internal")
    w1l1_bf = nc.dram_tensor("w1l1_bf", [8, 32, 128, 1024], BF16, kind="Internal")
    w2l1_bf = nc.dram_tensor("w2l1_bf", [8, 32, 128, 1024], BF16, kind="Internal")
    wa1_bf = nc.dram_tensor("wa1_bf", [1, 32, 128, 1024], BF16, kind="Internal")
    wa2_bf = nc.dram_tensor("wa2_bf", [1, 32, 128, 1024], BF16, kind="Internal")

    tap_outs = {}

    def tap(name, shape, dtype=F32):
        t = nc.dram_tensor("tap_" + name, shape, dtype, kind="ExternalOutput")
        tap_outs[name] = t
        return t

    def dump_sbuf(name, sb_ap, shape=None, dtype=None):
        if name not in taps:
            return
        dtype = dtype or sb_ap.dtype
        shape = shape or list(sb_ap.shape)
        t = tap(name, shape, dtype)
        sD.dinc(SY.dma_start(t.ap(), sb_ap))

    id_bf = nc.alloc_sbuf_tensor("id_bf", [128, 128], BF16)
    id_f = nc.alloc_sbuf_tensor("id_f", [128, 128], F32)
    id16 = nc.alloc_sbuf_tensor("id16", [16, 16], F32)
    ones_r = nc.alloc_sbuf_tensor("ones_r", [1, 128], F32)
    ones_c = nc.alloc_sbuf_tensor("ones_c", [128, 1], F32)
    ones64 = nc.alloc_sbuf_tensor("ones64", [128, 64], BF16)
    triUq = nc.alloc_sbuf_tensor("triUq", [128, 128], F32)
    triLq = nc.alloc_sbuf_tensor("triLq", [128, 128], F32)
    mNew = nc.alloc_sbuf_tensor("mNew", [128, 128], BF16)
    mOld = nc.alloc_sbuf_tensor("mOld", [128, 128], BF16)
    iota8 = nc.alloc_sbuf_tensor("iota8", [128, 8], F32)
    col8 = nc.alloc_sbuf_tensor("col8", [128, 8], F32)
    c_zeps = nc.alloc_sbuf_tensor("c_zeps", [128, 1], F32)
    zero40 = nc.alloc_sbuf_tensor("zero40", [128, 40], F32)

    for t in (id_bf, id_f):
        GP.memset(t[:], 0.0)
        GP.affine_select(out=t[:], in_=t[:], compare_op=ALU.not_equal,
                         fill=1.0, base=0, pattern=[[-1, 128]],
                         channel_multiplier=1)
    GP.memset(id16[:], 0.0)
    GP.affine_select(out=id16[:], in_=id16[:], compare_op=ALU.not_equal,
                     fill=1.0, base=0, pattern=[[-1, 16]], channel_multiplier=1)
    GP.memset(ones_r[:], 1.0)
    GP.memset(ones_c[:], 1.0)
    GP.memset(ones64[:], 1.0)
    GP.memset(zero40[:], 0.0)
    GP.memset(c_zeps[:], 1e-30)
    GP.memset(triUq[:], 1.0)
    GP.affine_select(out=triUq[:], in_=triUq[:], compare_op=ALU.is_ge,
                     fill=0.0, base=0, pattern=[[1, 128]], channel_multiplier=-1)
    GP.memset(triLq[:], 1.0)
    GP.affine_select(out=triLq[:], in_=triLq[:], compare_op=ALU.is_ge,
                     fill=0.0, base=0, pattern=[[-1, 128]], channel_multiplier=1)
    GP.memset(mNew[:], 1.0)
    GP.affine_select(out=mNew[:], in_=mNew[:], compare_op=ALU.is_ge,
                     fill=0.0, base=0, pattern=[[1, 128]], channel_multiplier=-1)
    GP.memset(mOld[:], 1.0)
    GP.affine_select(out=mOld[:], in_=mOld[:], compare_op=ALU.is_ge,
                     fill=0.0, base=0, pattern=[[-1, 128]], channel_multiplier=1)
    GP.iota(iota8[:], [[1, 8]], channel_multiplier=0,
            allow_small_or_imprecise_dtypes=True)

    l0eb = nc.alloc_sbuf_tensor("l0eb_sb", [128, RK], F32)
    l1eb = nc.alloc_sbuf_tensor("l1eb_sb", [128, RQ], F32)
    invs16 = nc.alloc_sbuf_tensor("invs16", [128, 16], F32)
    c16 = nc.alloc_sbuf_tensor("c16", [128, 16], F32)

    sD.dinc(SY.dma_start(l0eb[:], l0eb_d.ap()))
    sD.dinc(SY.dma_start(l1eb[:], l1eb_d.ap()))
    for hd in range(16):
        sD.dinc(SY.dma_start(
            bass.AP(invd_d, hd * 5120, [[40, 128], [1, 40]]), zero40[:]))
    SY.wait_ge(sD.h, sD.n)
    nc.all_engine_barrier()

    def load_qk_weights(es):
        wqc = es.enter_context(nc.sbuf_tensor(None, [128, 8 * D], BF16,
                                              allow_name_mangling=True))
        wkc = es.enter_context(nc.sbuf_tensor(None, [128, 8 * D], BF16,
                                              allow_name_mangling=True))
        bq_r = es.enter_context(nc.sbuf_tensor(None, [1, D], F32,
                                               allow_name_mangling=True))
        bk_r = es.enter_context(nc.sbuf_tensor(None, [1, D], F32,
                                               allow_name_mangling=True))
        wrmsq = es.enter_context(nc.sbuf_tensor(None, [128, 128], F32,
                                                allow_name_mangling=True))
        wrmsk = es.enter_context(nc.sbuf_tensor(None, [128, 128], F32,
                                                allow_name_mangling=True))
        rq_r = es.enter_context(nc.sbuf_tensor(None, [1, 64], F32,
                                               allow_name_mangling=True))
        rk_r = es.enter_context(nc.sbuf_tensor(None, [1, 64], F32,
                                               allow_name_mangling=True))
        for fc in range(8):
            sG.dinc(GP.dma_start(wqc[:, fc * D:(fc + 1) * D],
                                 IN["L0_attn_w_q"].ap()[128 * fc:128 * (fc + 1), :]))
            sG.dinc(GP.dma_start(wkc[:, fc * D:(fc + 1) * D],
                                 IN["L0_attn_w_k"].ap()[128 * fc:128 * (fc + 1), :]))
        sD.dinc(SY.dma_start(bq_r[:], IN["L0_attn_b_q"].ap()))
        sD.dinc(SY.dma_start(bk_r[:], IN["L0_attn_b_k"].ap()))
        sD.dinc(SY.dma_start(rq_r[:], IN["L0_attn_rms_q_w"].ap()))
        sD.dinc(SY.dma_start(rk_r[:], IN["L0_attn_rms_k_w"].ap()))
        with nc.psum_tensor(None, [128, 64], F32,
                            allow_name_mangling=True) as p_rms:
            SY.wait_ge(sD.h, sD.n)
            GP.wait_ge(sG.h, sG.n)
            nc.all_engine_barrier()
            TE.matmul(p_rms[:], ones_r[:], rq_r[:], start=True, stop=True)
            nc.all_engine_barrier()
            VE.tensor_copy(wrmsq[:, 0:64], p_rms[:])
            VE.tensor_copy(wrmsq[:, 64:128], p_rms[:])
            nc.all_engine_barrier()
            TE.matmul(p_rms[:], ones_r[:], rk_r[:], start=True, stop=True)
            nc.all_engine_barrier()
            VE.tensor_copy(wrmsk[:, 0:64], p_rms[:])
            VE.tensor_copy(wrmsk[:, 64:128], p_rms[:])
        nc.all_engine_barrier()
        return wqc, wkc, bq_r, bk_r, wrmsq, wrmsk

    def vf(ins):
        VE.wait_ge(sV.h, sV.inc(ins))

    def af(ins):
        AE.wait_ge(sA.h, sA.inc(ins))

    def ln_stats_apply(xt, out_ap, scr, cols, eps=EPS_LN, gb=None, bb=None,
                      second_out=None):
        VE.reduce_sum(col8[:, 0:1], xt, axis=AX.X)
        vf(VE.scalar_tensor_tensor(scr, xt, 1.0, xt, ALU.mult, ALU.mult,
                                   accum_out=col8[:, 1:2]))
        VE.tensor_scalar(col8[:, 2:3], col8[:, 0:1], 1.0 / cols, None, ALU.mult)
        VE.tensor_scalar(col8[:, 5:6], col8[:, 0:1], -1.0 / cols, None, ALU.mult)
        vf(VE.tensor_scalar(col8[:, 1:2], col8[:, 1:2], 1.0 / cols, eps,
                            ALU.mult, ALU.add))
        mV = sV.inc(VE.scalar_tensor_tensor(col8[:, 1:2], col8[:, 5:6],
                                            col8[:, 2:3], col8[:, 1:2],
                                            ALU.mult, ALU.add))
        AE.wait_ge(sV.h, mV)
        af(AE.activation(col8[:, 4:5], col8[:, 1:2], AF.Ln))
        mA = sA.inc(AE.activation(col8[:, 3:4], col8[:, 4:5], AF.Exp,
                                  scale=-0.5))
        VE.wait_ge(sA.h, mA)
        ins = VE.tensor_scalar(out_ap, xt, col8[:, 2:3], col8[:, 3:4],
                               ALU.subtract, ALU.mult)
        if gb is not None:
            vf(ins)
            vf(VE.tensor_tensor(out_ap, out_ap, gb, ALU.mult))
            ins = VE.tensor_tensor(out_ap, out_ap, bb, ALU.add)
        if second_out is not None:
            vf(ins)
            ins = VE.tensor_copy(second_out, out_ap)
        return sV.inc(ins)

    def fm_transpose(src_bf, fm_list, col_off, ptr, marks, src_mark):
        for fc in range(8):
            p = ptr[fc % 2]
            wv = max(src_mark, marks.get(fc % 2, 0))
            TE.wait_ge(sV.h, wv)
            mT = sT.inc(TE.transpose(p[:], src_bf[:, 128 * fc:128 * (fc + 1)],
                                     id_bf[:]))
            VE.wait_ge(sT.h, mT)
            marks[fc % 2] = sV.inc(VE.tensor_copy(
                fm_list[fc][:, col_off:col_off + 128], p[:]))


    def emit_moe(tag, qtiles, xnfm_list, xn_col0, estar_sb, g_acc_sb,
                 w1bf, w2bf, b1_in, b2_in, act):
        ttbs = []
        q0 = 0
        while q0 < qtiles:
            nq = min(6, qtiles - q0)
            ttbs.append((q0, nq))
            q0 += nq
        with ExitStack() as es:
            w1b = [es.enter_context(nc.sbuf_tensor(f"{tag}w1b{i}", [128, 1024], BF16))
                   for i in range(2)]
            w2b = [es.enter_context(nc.sbuf_tensor(f"{tag}w2b{i}", [128, 512], BF16))
                   for i in range(2)]
            hfm = es.enter_context(nc.sbuf_tensor(f"{tag}hfm", [128, 32 * 768], BF16))
            b1sb = es.enter_context(nc.sbuf_tensor(f"{tag}b1sb", [128, 32], F32))
            b2r = es.enter_context(nc.sbuf_tensor(f"{tag}b2r", [1, D], F32))
            mc6 = es.enter_context(nc.sbuf_tensor(f"{tag}mc6", [128, 6], F32))
            hps = es.enter_context(nc.psum_tensor(f"{tag}hps", [128, 768], F32))
            pw6 = [es.enter_context(nc.psum_tensor(f"{tag}pw{i}", [128, 512], F32))
                   for i in range(6)]
            sV.inc(VE.memset(g_acc_sb[:], 0.0))
            nc.all_engine_barrier()
            pw_free = None
            b_free_A = None
            b_free_T = None
            prev_w1_last = [None]
            prev_w2_last = [None]
            for (q0, nq) in ttbs:
                ncols = 128 * nq
                xcol = xn_col0 + 128 * q0
                segs = [(0, min(512, ncols))]
                if ncols > 512:
                    segs.append((512, ncols - 512))
                for e in range(8):
                    if b_free_A is not None:
                        SY.wait_ge(sA.h, b_free_A)
                    if b_free_T is not None:
                        SY.wait_ge(sT.h, b_free_T)
                    if prev_w1_last[0] is not None:
                        SY.wait_ge(sT.h, prev_w1_last[0])
                    mB = sD.dinc(SY.dma_start(b1sb[:], b1_in.ap()[e, :, :]))
                    mB2 = sD.dinc(SY.dma_start(b2r[:], b2_in.ap()[e, 0:1, :]))
                    ins = None
                    for t in range(nq):
                        ins = VE.tensor_scalar(mc6[:, t:t + 1],
                                               estar_sb[:, q0 + t:q0 + t + 1],
                                               float(e), None, ALU.is_equal)
                    vf(ins)
                    w1_mark = {}
                    gel = {}
                    for hc in range(32):
                        if w1_mark.get(hc - 2) is not None:
                            SY.wait_ge(sT.h, w1_mark[hc - 2])
                        elif prev_w1_last[0] is not None:
                            SY.wait_ge(sT.h, prev_w1_last[0])
                        mW = sD.dinc(SY.dma_start(
                            w1b[hc % 2][:],
                            bass.AP(w1bf, (e * 32 + hc) * 131072,
                                    [[1024, 128], [1, 1024]])))
                        TE.wait_ge(sD.h, mW)
                        if gel.get(hc - 1) is not None:
                            TE.wait_ge(sA.h, gel[hc - 1])
                        elif b_free_A is not None:
                            TE.wait_ge(sA.h, b_free_A)
                        ins = None
                        for (a, w) in segs:
                            for fc in range(8):
                                ins = TE.matmul(
                                    hps[:, a:a + w],
                                    w1b[hc % 2][:, 128 * fc:128 * (fc + 1)],
                                    xnfm_list[fc][:, xcol + a:xcol + a + w],
                                    start=(fc == 0), stop=(fc == 7))
                        w1_mark[hc] = sT.inc(ins)
                        AE.wait_ge(sT.h, w1_mark[hc])
                        AE.wait_ge(sD.h, mB)
                        gel[hc] = sA.inc(AE.activation(
                            hfm[:, 768 * hc:768 * hc + ncols], hps[:, 0:ncols],
                            act, bias=b1sb[:, hc:hc + 1]))
                    prev_w1_last[0] = w1_mark[31]
                    b_free_A = gel[31]
                    TE.wait_ge(sA.h, gel[31])
                    last_stt = None
                    for odh in range(2):
                        w2_mark = {}
                        for hc in range(32):
                            if w2_mark.get(hc - 2) is not None:
                                SY.wait_ge(sT.h, w2_mark[hc - 2])
                            elif prev_w2_last[0] is not None:
                                SY.wait_ge(sT.h, prev_w2_last[0])
                            mW = sD.dinc(SY.dma_start(
                                w2b[hc % 2][:],
                                bass.AP(w2bf, (e * 32 + hc) * 131072 + 512 * odh,
                                        [[1024, 128], [1, 512]])))
                            TE.wait_ge(sD.h, mW)
                            if hc == 0 and pw_free is not None:
                                TE.wait_ge(sV.h, pw_free)
                            ins = None
                            for t in range(nq):
                                ins = TE.matmul(
                                    pw6[t][:],
                                    hfm[:, 768 * hc + 128 * t:
                                        768 * hc + 128 * (t + 1)],
                                    w2b[hc % 2][:],
                                    start=(hc == 0), stop=False)
                            w2_mark[hc] = sT.inc(ins)
                        TE.wait_ge(sD.h, mB2)
                        prev_w2_last[0] = w2_mark[31]
                        ins = None
                        for t in range(nq):
                            ins = TE.matmul(pw6[t][:], ones_r[:],
                                            b2r[0:1, 512 * odh:512 * (odh + 1)],
                                            start=False, stop=True)
                        mE = sT.inc(ins)
                        b_free_T = mE
                        VE.wait_ge(sT.h, mE)
                        for t in range(nq):
                            gsl = g_acc_sb[:, D * (q0 + t) + 512 * odh:
                                           D * (q0 + t) + 512 * (odh + 1)]
                            last_stt = VE.scalar_tensor_tensor(
                                gsl, pw6[t][:], mc6[:, t:t + 1], gsl,
                                ALU.mult, ALU.add)
                        pw_free = sV.inc(last_stt)

    from contextlib import ExitStack
    with ExitStack() as _es1:
        wqc, wkc, bq_r, bk_r, wrmsq, wrmsk = load_qk_weights(_es1)
        xnSfm_t = _es1.enter_context(nc.sbuf_tensor("xnSfm", [128, 8 * SS_ROWS], BF16))
        xt0 = _es1.enter_context(nc.sbuf_tensor("xt0", [128, D], F32))
        xt1 = _es1.enter_context(nc.sbuf_tensor("xt1", [128, D], F32))
        scr = _es1.enter_context(nc.sbuf_tensor("scr", [128, D], F32))
        xnbf = _es1.enter_context(nc.sbuf_tensor("xnbf", [128, D], BF16))
        xnSfm = [xnSfm_t[:, c * SS_ROWS:(c + 1) * SS_ROWS] for c in range(8)]
        xts = [xt0, xt1]
        ld_mark = {}
        tr_marks = {}
        _esA = ExitStack()
        ptr0 = _esA.enter_context(nc.psum_tensor("ptr0", [128, 128], BF16))
        ptr1 = _esA.enter_context(nc.psum_tensor("ptr1", [128, 128], BF16))
        for s in range(40):
            xt = xts[s % 2]
            if s >= 2:
                SY.wait_ge(sV.h, ld_mark[s - 2])
            mD = sD.dinc(SY.dma_start(xt[:], xstat_d.ap()[128 * s:128 * (s + 1), :]))
            VE.wait_ge(sD.h, mD)
            mL = ln_stats_apply(xt[:], xnbf[:], scr[:], D)
            ld_mark[s] = mL
            fm_transpose(xnbf, xnSfm, 128 * s, (ptr0, ptr1), tr_marks, mL)
        nc.all_engine_barrier()
        _esA.close()
        if "xnS" in taps:
            t = tap("xnS", [128, 8 * SS_ROWS], BF16)
            sD.dinc(SY.dma_start(t.ap(), xnSfm_t[:]))
            SY.wait_ge(sD.h, sD.n)
            nc.all_engine_barrier()

        with ExitStack() as _es2:
            qS = _es2.enter_context(nc.sbuf_tensor("qS", [128, 4096], BF16))
            kS = _es2.enter_context(nc.sbuf_tensor("kS", [128, 4608], BF16))
            ktm = _es2.enter_context(nc.sbuf_tensor("ktm", [128, 128], BF16))
            sigscr = _es2.enter_context(nc.sbuf_tensor("sigscr", [128, 640], BF16))
            sigS = _es2.enter_context(nc.sbuf_tensor("sigS", [128, 64], F32))
            invd32 = _es2.enter_context(nc.sbuf_tensor("invd32", [128, 64], F32))
            pq0 = _es2.enter_context(nc.psum_tensor("pq0", [128, 128], F32))
            pq1 = _es2.enter_context(nc.psum_tensor("pq1", [128, 128], F32))
            ptq0 = _es2.enter_context(nc.psum_tensor("ptq0", [128, 128], BF16))
            ptq1 = _es2.enter_context(nc.psum_tensor("ptq1", [128, 128], BF16))
            psc0 = _es2.enter_context(nc.psum_tensor("psc0", [128, 640], F32))
            psc1 = _es2.enter_context(nc.psum_tensor("psc1", [128, 640], F32))
            pqs = [pq0, pq1]
            pts = [ptq0, ptq1]
            ev_mark = {}
            tp_mark = {}
            i_glob = [0]
            u_glob = [0]
            sig_mark = {}
            for hp in range(8):

                def proj_rms(st, wc, b_r, wrms, dst, dcol, i):
                    pq = pqs[i % 2]
                    pt = pts[i % 2]
                    if ev_mark.get(i % 2) is not None:
                        TE.wait_ge(sV.h, ev_mark[i % 2])
                    for fc in range(8):
                        TE.matmul(pq[:], xnSfm[fc][:, 128 * st:128 * (st + 1)],
                                  wc[:, fc * D + 128 * hp:fc * D + 128 * hp + 128],
                                  start=(fc == 0), stop=False)
                    mT = sT.inc(TE.matmul(pq[:], ones_r[:],
                                          b_r[0:1, 128 * hp:128 * hp + 128],
                                          start=False, stop=True))
                    AE.wait_ge(sT.h, mT)
                    AE.activation(scr[:, 0:64], pq[:, 0:64], AF.Square,
                                  accum_out=col8[:, 0:1])
                    mA0 = sA.inc(AE.activation(scr[:, 64:128], pq[:, 64:128],
                                               AF.Square,
                                               accum_out=col8[:, 1:2]))
                    VE.wait_ge(sA.h, mA0)
                    mV = sV.inc(VE.tensor_scalar(col8[:, 0:2], col8[:, 0:2],
                                                 1.0 / 64, EPS_RMS,
                                                 ALU.mult, ALU.add))
                    AE.wait_ge(sV.h, mV)
                    af(AE.activation(col8[:, 2:4], col8[:, 0:2], AF.Ln))
                    mA = sA.inc(AE.activation(col8[:, 4:6], col8[:, 2:4],
                                              AF.Exp, scale=-0.5))
                    VE.wait_ge(sA.h, mA)
                    VE.tensor_scalar(ktm[:, 0:64], pq[:, 0:64], col8[:, 4:5],
                                     None, ALU.mult)
                    vf(VE.tensor_scalar(ktm[:, 64:128], pq[:, 64:128],
                                        col8[:, 5:6], None, ALU.mult))
                    ev_mark[i % 2] = sV.inc(
                        VE.tensor_tensor(ktm[:], ktm[:], wrms, ALU.mult))
                    if tp_mark.get(i % 2) is not None:
                        TE.wait_ge(sV.h, tp_mark[i % 2])
                    TE.wait_ge(sV.h, ev_mark[i % 2])
                    mT2 = sT.inc(TE.transpose(pt[:], ktm[:], id_bf[:]))
                    VE.wait_ge(sT.h, mT2)
                    tp_mark[i % 2] = sV.inc(VE.tensor_copy(
                        dst[:, dcol:dcol + 128], pt[:]))

                for st in range(SQ0, SQ1):
                    proj_rms(st, wqc, bq_r, wrmsq[:], qS, 128 * (st - SQ0),
                             i_glob[0])
                    i_glob[0] += 1
                for st in range(8, 40):
                    proj_rms(st, wkc, bk_r, wrmsk[:], kS, 128 * (st - 4),
                             i_glob[0])
                    i_glob[0] += 1
                mz = sV.inc(VE.memset(kS[:, 0:512], 0.0))

                pscs = [psc0, psc1]
                for head01 in range(2):
                    hb = 64 * head01
                    for st in range(SQ0, SQ1):
                        u = u_glob[0]
                        psc = pscs[u % 2]
                        if sig_mark.get(u - 2) is not None:
                            TE.wait_ge(sA.h, sig_mark[u - 2])
                        TE.wait_ge(sV.h, mz)
                        qcol = 128 * (st - SQ0)
                        kcol = 128 * st - 1024
                        TE.matmul(psc[:, 0:512],
                                  qS[hb:hb + 64, qcol:qcol + 128],
                                  kS[hb:hb + 64, kcol:kcol + 512],
                                  start=True, stop=True)
                        mT = sT.inc(TE.matmul(psc[:, 512:640],
                                              qS[hb:hb + 64, qcol:qcol + 128],
                                              kS[hb:hb + 64, kcol + 512:kcol + 640],
                                              start=True, stop=True))
                        VE.wait_ge(sT.h, mT)
                        VE.tensor_tensor(psc[:, 0:128], psc[:, 0:128],
                                         triUq[:], ALU.mult)
                        mV = sV.inc(VE.tensor_tensor(psc[:, 512:640],
                                                     psc[:, 512:640],
                                                     triLq[:], ALU.mult))
                        AE.wait_ge(sV.h, mV)
                        sc = 32 * head01 + st - SQ0
                        sig_mark[u] = sA.inc(AE.activation(
                            sigscr[:], psc[:], AF.Sigmoid, scale=0.125,
                            accum_out=sigS[:, sc:sc + 1]))
                        u_glob[0] += 1
                    mA = sA.n
                    VE.wait_ge(sA.h, mA)
                    sl = sigS[:, 32 * head01:32 * head01 + 32]
                    dv = invd32[:, 32 * head01:32 * head01 + 32]
                    vf(VE.tensor_scalar(dv, sl, 1.0 / 4096, 0.921875,
                                        ALU.mult, ALU.add))
                    vf(VE.reciprocal(dv, dv))
                    head = 2 * hp + head01
                    mV = sV.inc(VE.reduce_sum(invs16[:, head:head + 1], dv,
                                              axis=AX.X))
                    AE.wait_ge(sV.h, mV)
                    SY.wait_ge(sV.h, mV)
                    sD.dinc(SY.dma_start(
                        bass.AP(invd_d, head * 5120 + 4, [[40, 128], [1, 32]]),
                        dv))
                if hp == tap_hp:
                    if "qS" in taps:
                        t = tap("qS", [128, 4096], BF16)
                        sD.dinc(SY.dma_start(t.ap(), qS[:]))
                    if "kS" in taps:
                        t = tap("kS", [128, 4608], BF16)
                        sD.dinc(SY.dma_start(t.ap(), kS[:]))
                    if "sigS" in taps:
                        t = tap("sigS", [128, 64], F32)
                        sD.dinc(SY.dma_start(t.ap(), sigS[:]))
                    if "invd32" in taps:
                        t = tap("invd32", [128, 64], F32)
                        sD.dinc(SY.dma_start(t.ap(), invd32[:]))
                    if taps:
                        SY.wait_ge(sD.h, sD.n)
                        nc.all_engine_barrier()

        nc.all_engine_barrier()
        with ExitStack() as _es3:
            psI = _es3.enter_context(nc.psum_tensor("psI", [16, 1], F32))
            psI2 = _es3.enter_context(nc.psum_tensor("psI2", [1, 16], F32))
            psB = _es3.enter_context(nc.psum_tensor("psB", [128, 16], F32))
            sI = _es3.enter_context(nc.sbuf_tensor("sI", [16, 1], F32))
            sIr = _es3.enter_context(nc.sbuf_tensor("sIr", [1, 16], F32))
            TE.matmul(psI[:], invs16[:], ones_c[:], start=True, stop=True)
            nc.all_engine_barrier()
            VE.tensor_copy(sI[:], psI[:])
            nc.all_engine_barrier()
            TE.transpose(psI2[:], sI[:], id16[:])
            nc.all_engine_barrier()
            VE.tensor_copy(sIr[:], psI2[:])
            nc.all_engine_barrier()
            TE.matmul(psB[:], ones_r[:], sIr[:], start=True, stop=True)
            nc.all_engine_barrier()
            vf(VE.tensor_scalar(c16[:], psB[:], 8.0, None, ALU.mult))
            VE.reciprocal(c16[:], c16[:])
            nc.all_engine_barrier()
        if "c16" in taps:
            t = tap("c16", [128, 16], F32)
            sD.dinc(SY.dma_start(t.ap(), c16[:]))
            SY.wait_ge(sD.h, sD.n)
        nc.all_engine_barrier()


    with ExitStack() as es2:
        xnfm_t = es2.enter_context(nc.sbuf_tensor("xnfm", [128, 8 * RX_ROWS], BF16))
        xnfm = [xnfm_t[:, c * RX_ROWS:(c + 1) * RX_ROWS] for c in range(8)]
        fhat = es2.enter_context(nc.sbuf_tensor("fhat", [128, 16 * RQ], F32))
        scr2 = es2.enter_context(nc.sbuf_tensor("scr2", [128, D], F32))
        v_tm = es2.enter_context(nc.sbuf_tensor("v_tm", [128, RK * 256], BF16))
        wvc = es2.enter_context(nc.sbuf_tensor("wvc", [128, 8 * 256], BF16))
        gwc = es2.enter_context(nc.sbuf_tensor("gwc", [128, 64], F32))
        bv_r = es2.enter_context(nc.sbuf_tensor("bv_r", [1, 256], F32))
        gb_r = es2.enter_context(nc.sbuf_tensor("gb_r", [1, 8], F32))
        estar = es2.enter_context(nc.sbuf_tensor("estar", [128, RQ], F32))
        dynoff_sb = es2.enter_context(nc.sbuf_tensor("dynoff_sb", [1, 2], I32))
        invr = es2.enter_context(nc.sbuf_tensor("invr", [128, 16 * RQ], F32))

        sG.dinc(GP.dma_start(dynoff_sb[:], dynoff_d.ap()))
        GP.wait_ge(sG.h, sG.n)
        with GP.register("groff") as groff:
            GP.reg_load(groff, dynoff_sb[0:1, 0:1])
            mGd = sG.dinc(GP.dma_start(
                invr[:], bass.AP(invd_d, groff,
                                 [[40, 128], [5120, 16], [1, RQ]])))
        VE.wait_ge(sG.h, mGd)
        ins = None
        for hd in range(16):
            ins = VE.tensor_scalar(fhat[:, RQ * hd:RQ * hd + RQ],
                                   invr[:, RQ * hd:RQ * hd + RQ],
                                   c16[:, hd:hd + 1], None, ALU.mult)
        sV.inc(ins)
        if "fhat" in taps:
            t = tap("fhat", [128, 16 * RQ], F32)
            SY.wait_ge(sV.h, sV.n)
            sD.dinc(SY.dma_start(t.ap(), fhat[:]))
            SY.wait_ge(sD.h, sD.n)
        nc.all_engine_barrier()

        with ExitStack() as esA:
            x2t = [esA.enter_context(nc.sbuf_tensor(f"x2t{i}", [128, D], F32))
                   for i in range(2)]
            xf2 = [esA.enter_context(nc.sbuf_tensor(f"xf2{i}", [128, D], F32))
                   for i in range(2)]
            xnb2 = esA.enter_context(nc.sbuf_tensor("xnb2", [128, D], BF16))
            ptrA = [esA.enter_context(nc.psum_tensor(f"ptrA{i}", [128, 128], BF16))
                    for i in range(2)]
            ap_mark = {}
            xf_mark = {}
            trm = {}
            for i in range(RK):
                xt = x2t[i % 2]
                if i >= 2:
                    SY.wait_ge(sV.h, ap_mark[i - 2])
                mD = sD.dinc(SY.dma_start(xt[:],
                                          xin_d.ap()[128 * i:128 * (i + 1), :]))
                VE.wait_ge(sD.h, mD)
                if i >= 2 and xf_mark.get(i - 2):
                    VE.wait_ge(sD.h, xf_mark[i - 2])
                mL = ln_stats_apply(xt[:], xf2[i % 2][:], scr2[:], D,
                                    second_out=xnb2[:])
                ap_mark[i] = mL
                if i >= 4:
                    SY.wait_ge(sV.h, mL)
                    xf_mark[i] = sD.dinc(SY.dma_start(
                        xnrel_d.ap()[128 * i:128 * (i + 1), :], xf2[i % 2][:]))
                fm_transpose(xnb2, xnfm, 128 * i, ptrA, trm, mL)
        nc.all_engine_barrier()

        for fc in range(8):
            sG.dinc(GP.dma_start(wvc[:, 256 * fc:256 * (fc + 1)],
                                 IN["L0_attn_w_v"].ap()[128 * fc:128 * (fc + 1), :]))
            sD.dinc(SY.dma_start(gwc[:, 8 * fc:8 * (fc + 1)],
                                 IN["L0_ffn_gate_w"].ap()[128 * fc:128 * (fc + 1), :]))
        sD.dinc(SY.dma_start(bv_r[:], IN["L0_attn_b_v"].ap()))
        sD.dinc(SY.dma_start(gb_r[:], IN["L0_ffn_gate_b"].ap()))
        SY.wait_ge(sD.h, sD.n)
        GP.wait_ge(sG.h, sG.n)
        nc.all_engine_barrier()

        with ExitStack() as esV:
            pv = [esV.enter_context(nc.psum_tensor(f"pv{i}", [128, 256], F32))
                  for i in range(2)]
            evm = {}
            for i in range(RK):
                if i >= 2:
                    TE.wait_ge(sV.h, evm[i - 2])
                for fc in range(8):
                    TE.matmul(pv[i % 2][:], xnfm[fc][:, 128 * i:128 * (i + 1)],
                              wvc[:, 256 * fc:256 * (fc + 1)],
                              start=(fc == 0), stop=False)
                mT = sT.inc(TE.matmul(pv[i % 2][:], ones_r[:], bv_r[:],
                                      start=False, stop=True))
                VE.wait_ge(sT.h, mT)
                evm[i] = sV.inc(VE.tensor_copy(
                    v_tm[:, 256 * i:256 * (i + 1)], pv[i % 2][:]))
        nc.all_engine_barrier()
        if "vtm" in taps:
            t = tap("vtm", [128, RK * 256], BF16)
            sD.dinc(SY.dma_start(t.ap(), v_tm[:]))
            SY.wait_ge(sD.h, sD.n)
            nc.all_engine_barrier()

        esAvn = ExitStack()
        avn_t = esAvn.enter_context(nc.sbuf_tensor("avn", [128, 8 * RQ * 128], BF16))
        with ExitStack() as esCv:
            wcv = [esCv.enter_context(nc.sbuf_tensor(f"wcv{i}", [128, 4096], BF16))
                   for i in range(2)]
            cv_wr = {}
            cv_i = [0]

            def convert_w1(src_ap_fn, dst_dram, e):
                for fc in range(8):
                    i = cv_i[0]
                    buf = wcv[i % 2]
                    if i >= 2:
                        GP.wait_ge(sD.h, cv_wr[i - 2])
                    mG = sG.dinc(GP.dma_start(buf[:], src_ap_fn(fc)))
                    SY.wait_ge(sG.h, mG)
                    cv_wr[i] = sD.dinc(SY.dma_start(
                        bass.AP(dst_dram,
                                e * 32 * 128 * 1024 + 128 * fc,
                                [[1024, 128], [131072, 32], [1, 128]]),
                        buf[:]))
                    cv_i[0] += 1

            for e in range(8):
                convert_w1(lambda fc, e=e: IN["L0_ffn_w1"].ap()
                           [e, 128 * fc:128 * (fc + 1), :], w1l0_bf, e)
            for e in range(8):
                sG.dinc(GP.dma_start(
                    bass.AP(w2l0_bf, e * 32 * 128 * 1024,
                            [[1024, 4096], [1, 1024]]),
                    IN["L0_ffn_w2"].ap()[e, :, :]))
            for e in range(8):
                convert_w1(lambda fc, e=e: IN["L1_ffn_w1"].ap()
                           [e, 128 * fc:128 * (fc + 1), :], w1l1_bf, e)
            for e in range(8):
                sG.dinc(GP.dma_start(
                    bass.AP(w2l1_bf, e * 32 * 128 * 1024,
                            [[1024, 4096], [1, 1024]]),
                    IN["L1_ffn_w2"].ap()[e, :, :]))
            convert_w1(lambda fc: IN["L1_attn_w1"].ap()
                       [128 * fc:128 * (fc + 1), :], wa1_bf, 0)
            sG.dinc(GP.dma_start(
                bass.AP(wa2_bf, 0, [[1024, 4096], [1, 1024]]),
                IN["L1_attn_w2"].ap()[:, :]))
            cvG = sG.n
            cvD = sD.n

            wqc, wkc, bq_r, bk_r, wrmsq, wrmsk = load_qk_weights(esCv)
            qhf = esCv.enter_context(nc.sbuf_tensor("qhf", [128, RQ * 128], BF16))
            khf = esCv.enter_context(nc.sbuf_tensor("khf", [128, RK * 128], BF16))
            ring_t = esCv.enter_context(nc.sbuf_tensor("ring", [128, 6 * 640], BF16))
            ktm2 = esCv.enter_context(nc.sbuf_tensor("ktm2", [128, 128], BF16))
            z1s = [esCv.enter_context(nc.sbuf_tensor(f"z1s{i}", [64, 128], F32))
                   for i in range(2)]
            rzbs = [esCv.enter_context(nc.sbuf_tensor(f"rzbs{i}", [64, 128], F32))
                    for i in range(2)]
            ring = [ring_t[:, 640 * i:640 * (i + 1)] for i in range(6)]

            i2g = [0]
            evq = {}
            tpq = {}
            uBg = [0]
            qBg = [0]
            expm = {}
            trim = {}
            avm = {}
            avnm = {}
            lnm = {}
            av_hist = []
            for hp in range(8):
                kv = hp // 2
                with ExitStack() as esP:
                    pq2 = [esP.enter_context(
                        nc.psum_tensor(f"pq2_{hp}_{i}", [128, 128], F32))
                        for i in range(2)]
                    pt2 = [esP.enter_context(
                        nc.psum_tensor(f"pt2_{hp}_{i}", [128, 128], BF16))
                        for i in range(2)]

                    def proj2(dst, dcol, rx, fcols, i, wc, b_r, wrms):
                        pq = pq2[i % 2]
                        pt = pt2[i % 2]
                        if evq.get(i % 2) is not None:
                            TE.wait_ge(sV.h, evq[i % 2])
                        for fc in range(8):
                            TE.matmul(pq[:], xnfm[fc][:, 128 * rx:128 * (rx + 1)],
                                      wc[:, fc * D + 128 * hp:fc * D + 128 * hp + 128],
                                      start=(fc == 0), stop=False)
                        mT = sT.inc(TE.matmul(pq[:], ones_r[:],
                                              b_r[0:1, 128 * hp:128 * hp + 128],
                                              start=False, stop=True))
                        AE.wait_ge(sT.h, mT)
                        AE.activation(scr2[:, 0:64], pq[:, 0:64], AF.Square,
                                      accum_out=col8[:, 0:1])
                        mA0 = sA.inc(AE.activation(scr2[:, 64:128], pq[:, 64:128],
                                                   AF.Square,
                                                   accum_out=col8[:, 1:2]))
                        VE.wait_ge(sA.h, mA0)
                        mV = sV.inc(VE.tensor_scalar(col8[:, 0:2], col8[:, 0:2],
                                                     1.0 / 64, EPS_RMS,
                                                     ALU.mult, ALU.add))
                        AE.wait_ge(sV.h, mV)
                        af(AE.activation(col8[:, 2:4], col8[:, 0:2], AF.Ln))
                        mA = sA.inc(AE.activation(col8[:, 4:6], col8[:, 2:4],
                                                  AF.Exp, scale=-0.5))
                        VE.wait_ge(sA.h, mA)
                        if fcols is not None:
                            VE.tensor_tensor(col8[:, 4:5], col8[:, 4:5],
                                             fcols[0], ALU.mult)
                            vf(VE.tensor_tensor(col8[:, 5:6], col8[:, 5:6],
                                                fcols[1], ALU.mult))
                        VE.tensor_scalar(ktm2[:, 0:64], pq[:, 0:64],
                                         col8[:, 4:5], None, ALU.mult)
                        vf(VE.tensor_scalar(ktm2[:, 64:128], pq[:, 64:128],
                                            col8[:, 5:6], None, ALU.mult))
                        evq[i % 2] = sV.inc(
                            VE.tensor_tensor(ktm2[:], ktm2[:], wrms, ALU.mult))
                        if tpq.get(i % 2) is not None:
                            TE.wait_ge(sV.h, tpq[i % 2])
                        TE.wait_ge(sV.h, evq[i % 2])
                        mT2 = sT.inc(TE.transpose(pt[:], ktm2[:], id_bf[:]))
                        VE.wait_ge(sT.h, mT2)
                        tpq[i % 2] = sV.inc(VE.tensor_copy(
                            dst[:, dcol:dcol + 128], pt[:]))

                    for qt in range(RQ):
                        proj2(qhf, 128 * qt, qt + 4,
                              (fhat[:, RQ * 2 * hp + qt:RQ * 2 * hp + qt + 1],
                               fhat[:, RQ * (2 * hp + 1) + qt:
                                     RQ * (2 * hp + 1) + qt + 1]),
                              i2g[0], wqc, bq_r, wrmsq[:])
                        i2g[0] += 1
                    for kt in range(RK):
                        proj2(khf, 128 * kt, kt, None, i2g[0], wkc, bk_r,
                              wrmsk[:])
                        i2g[0] += 1
                nc.all_engine_barrier()
                if hp == tap_hp and "qrel" in taps:
                    t = tap("qrel", [128, RQ * 128], BF16)
                    sD.dinc(SY.dma_start(t.ap(), qhf[:]))
                    t = tap("krel", [128, RK * 128], BF16)
                    tap_outs["krel"] = t
                    sD.dinc(SY.dma_start(t.ap(), khf[:]))
                    SY.wait_ge(sD.h, sD.n)
                    nc.all_engine_barrier()

                with ExitStack() as esB:
                    psc = [esB.enter_context(
                        nc.psum_tensor(f"psc2_{hp}_{i}", [128, 640], F32))
                        for i in range(2)]
                    av2 = [esB.enter_context(
                        nc.psum_tensor(f"av2_{hp}_{i}", [64, 128], F32))
                        for i in range(2)]
                    zb2 = [esB.enter_context(
                        nc.psum_tensor(f"zb2_{hp}_{i}", [64, 128], F32))
                        for i in range(2)]
                    for h01 in range(2):
                        hb = 64 * h01
                        for J in range(RK):
                            u = uBg[0]
                            uBg[0] += 1
                            s_lo = max(0, 4 - J)
                            s_hi = min(4, RQ + 3 - J)
                            c_lo, c_hi = 128 * s_lo, 128 * (s_hi + 1)
                            pb = psc[u % 2]
                            if expm.get(u - 2) is not None:
                                TE.wait_ge(sA.h, expm[u - 2])
                            segs = []
                            if c_lo < 512:
                                segs.append((c_lo, min(c_hi, 512)))
                            if c_hi > 512:
                                segs.append((max(c_lo, 512), c_hi))
                            for (a, b_) in segs:
                                q0 = 128 * (J - 4) + a
                                ins = TE.matmul(
                                    pb[:, a:b_],
                                    khf[hb:hb + 64, 128 * J:128 * (J + 1)],
                                    qhf[hb:hb + 64, q0:q0 + (b_ - a)],
                                    start=True, stop=True)
                            mQK = sT.inc(ins)
                            g = None
                            for (uu, mm_) in reversed(av_hist):
                                if uu <= u - 2:
                                    g = mm_
                                    break
                            if g is not None:
                                AE.wait_ge(sT.h, g)
                            AE.wait_ge(sT.h, mQK)
                            expm[u] = sA.inc(AE.activation(
                                ring[u % 6][:, c_lo:c_hi], pb[:, c_lo:c_hi],
                                AF.Exp, bias=l0eb[:, J:J + 1]))
                            VE.wait_ge(sA.h, expm[u])
                            ins = None
                            if J >= 4:
                                ins = VE.tensor_tensor(ring[u % 6][:, 0:128],
                                                       ring[u % 6][:, 0:128],
                                                       mNew[:], ALU.mult)
                            if J <= RQ - 1:
                                ins = VE.tensor_tensor(ring[u % 6][:, 512:640],
                                                       ring[u % 6][:, 512:640],
                                                       mOld[:], ALU.mult)
                            trim[u] = sV.inc(ins) if ins is not None else sV.n
                            q = J - 4
                            if 0 <= q < RQ:
                                qc = qBg[0]
                                qBg[0] += 1
                                TE.wait_ge(sV.h, trim[u])
                                if avnm.get(qc - 2) is not None:
                                    TE.wait_ge(sV.h, avnm[qc - 2])
                                if lnm.get(qc - 2) is not None:
                                    TE.wait_ge(sA.h, lnm[qc - 2])
                                pa = av2[qc % 2]
                                pz = zb2[qc % 2]
                                for c in range(5):
                                    TE.matmul(pa[:],
                                              v_tm[:, 256 * (q + c) + 64 * kv:
                                                   256 * (q + c) + 64 * kv + 64],
                                              ring[(u - 4 + c) % 6][:, 128 * (4 - c):
                                                                    128 * (5 - c)],
                                              start=(c == 0), stop=(c == 4))
                                for c in range(5):
                                    ins = TE.matmul(
                                        pz[:], ones64[:],
                                        ring[(u - 4 + c) % 6][:, 128 * (4 - c):
                                                              128 * (5 - c)],
                                        start=(c == 0), stop=(c == 4))
                                avm[u] = sT.inc(ins)
                                av_hist.append((u, avm[u]))
                                if len(av_hist) > 8:
                                    av_hist.pop(0)
                                AE.wait_ge(sT.h, avm[u])
                                af(AE.activation(z1s[qc % 2][:], pz[:], AF.Ln,
                                                 bias=c_zeps[0:64, :]))
                                lnm[qc] = sA.inc(AE.activation(
                                    rzbs[qc % 2][:], z1s[qc % 2][:], AF.Exp,
                                    scale=-1.0))
                                VE.wait_ge(sA.h, lnm[qc])
                                avnm[qc] = sV.inc(VE.tensor_tensor(
                                    avn_t[:, 2816 * hp + 128 * q:
                                          2816 * hp + 128 * (q + 1)]
                                    [hb:hb + 64, :],
                                    pa[:], rzbs[qc % 2][:], ALU.mult))
                nc.all_engine_barrier()
        if "avn" in taps:
            t = tap("avn", [128, 8 * RQ * 128], BF16)
            sD.dinc(SY.dma_start(t.ap(), avn_t[:]))
            SY.wait_ge(sD.h, sD.n)
            nc.all_engine_barrier()

        with ExitStack() as esO:
            pw = [esO.enter_context(nc.psum_tensor(f"pwo{i}", [128, D], F32))
                  for i in range(2)]
            sf = [esO.enter_context(nc.sbuf_tensor(f"sf{i}", [128, D], F32))
                  for i in range(2)]
            botile = esO.enter_context(nc.sbuf_tensor("botile", [128, D], F32))
            woc = esO.enter_context(nc.sbuf_tensor("woc", [128, 8 * D], BF16))
            bo_r = esO.enter_context(nc.sbuf_tensor("bo_r", [1, D], F32))
            for fc in range(8):
                sG.dinc(GP.dma_start(woc[:, D * fc:D * (fc + 1)],
                                     IN["L0_attn_w_o"].ap()[128 * fc:128 * (fc + 1), :]))
            sD.dinc(SY.dma_start(bo_r[:], IN["L0_attn_b_o"].ap()))
            GP.wait_ge(sG.h, sG.n)
            SY.wait_ge(sD.h, sD.n)
            nc.all_engine_barrier()
            TE.matmul(pw[0][:, 0:512], ones_r[:], bo_r[0:1, 0:512],
                      start=True, stop=True)
            mT = sT.inc(TE.matmul(pw[0][:, 512:1024], ones_r[:],
                                  bo_r[0:1, 512:1024], start=True, stop=True))
            VE.wait_ge(sT.h, mT)
            mV = sV.inc(VE.tensor_copy(botile[:], pw[0][:]))
            TE.wait_ge(sV.h, mV)
            evo = {}
            for qt in range(RQ):
                pq_ = pw[qt % 2]
                if evo.get(qt - 2) is not None:
                    TE.wait_ge(sV.h, evo[qt - 2])
                for half in range(2):
                    for hp in range(8):
                        ins = TE.matmul(
                            pq_[:, 512 * half:512 * (half + 1)],
                            avn_t[:, 2816 * hp + 128 * qt:
                                  2816 * hp + 128 * (qt + 1)],
                            woc[:, D * hp + 512 * half:D * hp + 512 * (half + 1)],
                            start=(hp == 0), stop=(hp == 7))
                mT = sT.inc(ins)
                VE.wait_ge(sT.h, mT)
                evo[qt] = sV.inc(VE.tensor_tensor(sf[qt % 2][:], pq_[:],
                                                  botile[:], ALU.add))
                SY.wait_ge(sV.h, evo[qt])
                sD.dinc(SY.dma_start(s0_d.ap()[128 * qt:128 * (qt + 1), :],
                                     sf[qt % 2][:]))
        nc.all_engine_barrier()
        if "s0" in taps:
            t = tap("s0", [RQ * 128, D], F32)
            sD.dinc(SY.dma_start(t.ap(), s0_d.ap()))
            SY.wait_ge(sD.h, sD.n)
            nc.all_engine_barrier()
        esAvn.close()
        g_acc = es2.enter_context(nc.sbuf_tensor("g_acc", [128, RQ * D], BF16))

        with ExitStack() as esG2:
            xg = [esG2.enter_context(nc.sbuf_tensor(f"xg{i}", [128, D], F32))
                  for i in range(2)]
            xT = [esG2.enter_context(nc.sbuf_tensor(f"xT{i}", [128, 128], F32))
                  for i in range(2)]
            ptf = [esG2.enter_context(nc.psum_tensor(f"ptf{i}", [128, 128], F32))
                   for i in range(2)]
            pg = esG2.enter_context(nc.psum_tensor("pg", [128, 8], F32))
            lg = esG2.enter_context(nc.sbuf_tensor("lg", [128, 8], F32))
            lg2 = esG2.enter_context(nc.sbuf_tensor("lg2", [128, 8], F32))
            eq8 = esG2.enter_context(nc.sbuf_tensor("eq8", [128, 8], F32))
            t8 = esG2.enter_context(nc.sbuf_tensor("t8", [128, 8], F32))
            xg_mark = {}
            tev = {}
            mmm = {}
            for qt in range(RQ):
                if xg_mark.get(qt - 2) is not None:
                    SY.wait_ge(sT.h, xg_mark[qt - 2])
                mD = sD.dinc(SY.dma_start(
                    xg[qt % 2][:],
                    xnrel_d.ap()[128 * (qt + 4):128 * (qt + 5), :]))
                for fc in range(8):
                    u = qt * 8 + fc
                    TE.wait_ge(sD.h, mD)
                    if tev.get(u - 2) is not None:
                        TE.wait_ge(sV.h, tev[u - 2])
                    mTr = sT.inc(TE.transpose(ptf[fc % 2][:],
                                              xg[qt % 2][:, 128 * fc:128 * (fc + 1)],
                                              id_f[:]))
                    VE.wait_ge(sT.h, mTr)
                    if mmm.get(u - 2) is not None:
                        VE.wait_ge(sT.h, mmm[u - 2])
                    mEv = sV.inc(VE.tensor_copy(xT[fc % 2][:], ptf[fc % 2][:]))
                    TE.wait_ge(sV.h, mEv)
                    mmm[u] = sT.inc(TE.matmul(pg[:], xT[fc % 2][:],
                                              gwc[:, 8 * fc:8 * (fc + 1)],
                                              start=(fc == 0), stop=False))
                    tev[u] = mEv
                xg_mark[qt] = sT.inc(TE.matmul(pg[:], ones_r[:], gb_r[:],
                                               start=False, stop=True))
                VE.wait_ge(sT.h, xg_mark[qt])
                vf(VE.tensor_copy(lg[:], pg[:]))
                vf(VE.reduce_max(col8[:, 0:1], lg[:], axis=AX.X))
                vf(VE.tensor_scalar(eq8[:], lg[:], col8[:, 0:1], None,
                                    ALU.is_equal))
                VE.tensor_tensor(t8[:], eq8[:], iota8[:], ALU.mult)
                vf(VE.scalar_tensor_tensor(lg2[:], eq8[:], -1e30, lg[:],
                                           ALU.mult, ALU.add))
                VE.reduce_max(col8[:, 1:2], t8[:], axis=AX.X)
                vf(VE.reduce_max(col8[:, 2:3], lg2[:], axis=AX.X))
                vf(VE.tensor_scalar(eq8[:], lg2[:], col8[:, 2:3], None,
                                    ALU.is_equal))
                vf(VE.tensor_tensor(t8[:], eq8[:], iota8[:], ALU.mult))
                vf(VE.reduce_max(col8[:, 3:4], t8[:], axis=AX.X))
                sV.inc(VE.tensor_tensor(estar[:, qt:qt + 1], col8[:, 1:2],
                                        col8[:, 3:4], ALU.max))
        nc.all_engine_barrier()
        if "estar" in taps:
            t = tap("estar", [128, RQ], F32)
            sD.dinc(SY.dma_start(t.ap(), estar[:]))
            SY.wait_ge(sD.h, sD.n)
            nc.all_engine_barrier()

        SY.wait_ge(sG.h, cvG)
        SY.wait_ge(sD.h, cvD)
        nc.all_engine_barrier()
        if "w1bf0" in taps:
            t = tap("w1bf0", [32 * 128, 1024], BF16)
            sD.dinc(SY.dma_start(t.ap(), bass.AP(w1l0_bf, 0,
                                                 [[1024, 4096], [1, 1024]])))
            t = tap("w2bf0", [32 * 128, 1024], BF16)
            sD.dinc(SY.dma_start(t.ap(), bass.AP(w2l0_bf, 0,
                                                 [[1024, 4096], [1, 1024]])))
            SY.wait_ge(sD.h, sD.n)
            nc.all_engine_barrier()
        emit_moe("L0", RQ, xnfm, 512, estar, g_acc, w1l0_bf, w2l0_bf,
                 IN["L0_ffn_b1"], IN["L0_ffn_b2"], AF.Gelu)
        nc.all_engine_barrier()
        if "gacc" in taps:
            t = tap("gacc", [128, RQ * D], BF16)
            sD.dinc(SY.dma_start(t.ap(), g_acc[:]))
            SY.wait_ge(sD.h, sD.n)
            nc.all_engine_barrier()

        with ExitStack() as esE:
            xc = [esE.enter_context(nc.sbuf_tensor(f"xc{i}", [128, D], F32))
                  for i in range(2)]
            sc = [esE.enter_context(nc.sbuf_tensor(f"sc{i}", [128, D], F32))
                  for i in range(2)]
            gs = [esE.enter_context(nc.sbuf_tensor(f"gs{i}", [128, D], F32))
                  for i in range(2)]
            x1t = [esE.enter_context(nc.sbuf_tensor(f"x1t{i}", [128, D], F32))
                   for i in range(2)]
            x1m = {}
            mVe = {}
            for qt in range(RQ):
                if x1m.get(qt - 2) is not None:
                    SY.wait_ge(sD.h, x1m[qt - 2])
                mD1 = sD.dinc(SY.dma_start(
                    xc[qt % 2][:],
                    xin_d.ap()[128 * (qt + 4):128 * (qt + 5), :]))
                mD2 = sD.dinc(SY.dma_start(
                    sc[qt % 2][:], s0_d.ap()[128 * qt:128 * (qt + 1), :]))
                if mVe.get(qt - 2) is not None:
                    AE.wait_ge(sV.h, mVe[qt - 2])
                mA = sA.inc(AE.activation(gs[qt % 2][:],
                                          g_acc[:, D * qt:D * (qt + 1)],
                                          AF.Silu, scale=2.0))
                VE.wait_ge(sA.h, mA)
                VE.wait_ge(sD.h, max(mD1, mD2))
                vf(VE.tensor_tensor(gs[qt % 2][:], sc[qt % 2][:], gs[qt % 2][:],
                                    ALU.mult))
                mV = sV.inc(VE.tensor_tensor(x1t[qt % 2][:], xc[qt % 2][:],
                                             gs[qt % 2][:], ALU.add))
                mVe[qt] = mV
                SY.wait_ge(sV.h, mV)
                x1m[qt] = sD.dinc(SY.dma_start(
                    x1_d.ap()[128 * qt:128 * (qt + 1), :], x1t[qt % 2][:]))
        SY.wait_ge(sD.h, sD.n)
        nc.all_engine_barrier()
        if "x1" in taps:
            t = tap("x1", [RQ * 128, D], F32)
            sD.dinc(SY.dma_start(t.ap(), x1_d.ap()))
            SY.wait_ge(sD.h, sD.n)
            nc.all_engine_barrier()

    with ExitStack() as es3:
        xn1fm_t = es3.enter_context(nc.sbuf_tensor("xn1fm", [128, 8 * 2048], BF16))
        xn1fm = [xn1fm_t[:, c * 2048:(c + 1) * 2048] for c in range(8)]
        scr3 = es3.enter_context(nc.sbuf_tensor("scr3", [128, D], F32))
        ctx_t = es3.enter_context(nc.sbuf_tensor("ctx", [128, 8 * 2048], BF16))
        esLn1 = ExitStack()
        l1afm_t = esLn1.enter_context(nc.sbuf_tensor("l1afm", [128, 8 * RQ * 128], BF16))
        l1afm = [l1afm_t[:, c * RQ * 128:(c + 1) * RQ * 128] for c in range(8)]
        g1t = esLn1.enter_context(nc.sbuf_tensor("g1t", [128, D], F32))
        b1t = esLn1.enter_context(nc.sbuf_tensor("b1t", [128, D], F32))

        with ExitStack() as esT:
            r1 = esT.enter_context(nc.sbuf_tensor("r1", [1, D], F32))
            r2 = esT.enter_context(nc.sbuf_tensor("r2", [1, D], F32))
            pb1 = esT.enter_context(nc.psum_tensor("pb1", [128, 512], F32))
            sD.dinc(SY.dma_start(r1[:], IN["L1_attn_ln1_g"].ap()))
            sD.dinc(SY.dma_start(r2[:], IN["L1_attn_ln1_b"].ap()))
            SY.wait_ge(sD.h, sD.n)
            nc.all_engine_barrier()
            for half in range(2):
                TE.matmul(pb1[:], ones_r[:], r1[0:1, 512 * half:512 * (half + 1)],
                          start=True, stop=True)
                nc.all_engine_barrier()
                VE.tensor_copy(g1t[:, 512 * half:512 * (half + 1)], pb1[:])
                nc.all_engine_barrier()
                TE.matmul(pb1[:], ones_r[:], r2[0:1, 512 * half:512 * (half + 1)],
                          start=True, stop=True)
                nc.all_engine_barrier()
                VE.tensor_copy(b1t[:, 512 * half:512 * (half + 1)], pb1[:])
                nc.all_engine_barrier()

        with ExitStack() as esA3:
            x3t = [esA3.enter_context(nc.sbuf_tensor(f"x3t{i}", [128, D], F32))
                   for i in range(2)]
            xn1f = [esA3.enter_context(nc.sbuf_tensor(f"xn1f{i}", [128, D], F32))
                    for i in range(2)]
            xn1b = esA3.enter_context(nc.sbuf_tensor("xn1b", [128, D], BF16))
            ln1b = esA3.enter_context(nc.sbuf_tensor("ln1bt", [128, D], BF16))
            ptr3 = [esA3.enter_context(nc.psum_tensor(f"ptr3{i}", [128, 128], BF16))
                    for i in range(2)]
            apm = {}
            dmm = {}
            trm3 = {}
            for i in range(RQ):
                xt = x3t[i % 2]
                if apm.get(i - 2) is not None:
                    SY.wait_ge(sV.h, apm[i - 2])
                mD = sD.dinc(SY.dma_start(xt[:],
                                          x1_d.ap()[128 * i:128 * (i + 1), :]))
                VE.wait_ge(sD.h, mD)
                if dmm.get(i - 2) is not None:
                    VE.wait_ge(sD.h, dmm[i - 2])
                own = OWN0 <= i < OWN0 + NOWN
                mL1 = ln_stats_apply(xt[:], xn1f[i % 2][:], scr3[:], D,
                                     second_out=xn1b[:] if own else None)
                SY.wait_ge(sV.h, mL1)
                dmm[i] = sD.dinc(SY.dma_start(
                    xn1_d.ap()[128 * i:128 * (i + 1), :], xn1f[i % 2][:]))
                if own:
                    fm_transpose(xn1b, xn1fm, 128 * (i - OWN0), ptr3, trm3, mL1)
                VE.wait_ge(sV.h, mL1)
                mL2 = ln_stats_apply(xn1f[i % 2][:], ln1b[:], scr3[:], D,
                                     gb=g1t[:], bb=b1t[:])
                apm[i] = mL2
                fm_transpose(ln1b, l1afm, 128 * i, ptr3, trm3, mL2)
        nc.all_engine_barrier()
        if "xn1" in taps:
            t = tap("xn1", [RQ * 128, D], F32)
            sD.dinc(SY.dma_start(t.ap(), xn1_d.ap()))
            SY.wait_ge(sD.h, sD.n)
            nc.all_engine_barrier()

        sV.inc(VE.memset(ctx_t[:], 0.0))
        nc.all_engine_barrier()
        with ExitStack() as esB3:
            gamb = esB3.enter_context(nc.sbuf_tensor("gamb", [128, NCHUNK * 512], BF16))
            with ExitStack() as esT2:
                gmr = esT2.enter_context(nc.sbuf_tensor("gmr", [1, NCHUNK * 512], F32))
                pb2 = esT2.enter_context(nc.psum_tensor("pb2", [128, 512], F32))
                sD.dinc(SY.dma_start(gmr[:], gam_d.ap()))
                SY.wait_ge(sD.h, sD.n)
                nc.all_engine_barrier()
                for j in range(NCHUNK):
                    TE.matmul(pb2[:], ones_r[:], gmr[0:1, 512 * j:512 * (j + 1)],
                              start=True, stop=True)
                    nc.all_engine_barrier()
                    VE.tensor_copy(gamb[:, 512 * j:512 * (j + 1)], pb2[:])
                    nc.all_engine_barrier()
            wq1 = esB3.enter_context(nc.sbuf_tensor("wq1", [128, 8 * 128], BF16))
            wk1 = esB3.enter_context(nc.sbuf_tensor("wk1", [128, 8 * 128], BF16))
            wv1 = esB3.enter_context(nc.sbuf_tensor("wv1", [128, 8 * 128], BF16))
            bqc = esB3.enter_context(nc.sbuf_tensor("bqc", [128, 8], F32))
            bkc = esB3.enter_context(nc.sbuf_tensor("bkc", [128, 8], F32))
            bv1r = esB3.enter_context(nc.sbuf_tensor("bv1r", [1, D], F32))
            q1f = esB3.enter_context(nc.sbuf_tensor("q1f", [128, RQ * 128], BF16))
            k1f = esB3.enter_context(nc.sbuf_tensor("k1f", [128, RQ * 128], BF16))
            v1t = esB3.enter_context(nc.sbuf_tensor("v1t", [128, RQ * 128], BF16))
            psb = [esB3.enter_context(nc.sbuf_tensor(f"psb{i}", [128, 512], BF16))
                   for i in range(4)]
            z5 = esB3.enter_context(nc.sbuf_tensor("z5", [64, 512], F32))
            rz5 = esB3.enter_context(nc.sbuf_tensor("rz5", [64, 512], F32))
            tm5 = esB3.enter_context(nc.sbuf_tensor("tm5", [128, 512], BF16))
            sD.dinc(SY.dma_start(bqc[:], l1bqc_d.ap()))
            sD.dinc(SY.dma_start(bkc[:], l1bkc_d.ap()))
            sD.dinc(SY.dma_start(bv1r[:], IN["L1_attn_bv"].ap()))
            with ExitStack() as esP3:
                pp3 = [esP3.enter_context(
                    nc.psum_tensor(f"pp3{i}", [128, 512], F32)) for i in range(2)]
                pk4 = [esP3.enter_context(
                    nc.psum_tensor(f"pk4{i}", [128, 512], F32)) for i in range(4)]
                avp = esP3.enter_context(nc.psum_tensor("avp", [64, 512], F32))
                zbp = esP3.enter_context(nc.psum_tensor("zbp", [64, 512], F32))
                prm = {}
                uP = [0]
                expm3 = {}
                avm3 = [None]
                ctxm = [None]
                lnm3 = [None]
                for hp in range(8):
                    mGw = None
                    for fc in range(8):
                        sG.dinc(GP.dma_start(
                            wq1[:, 128 * fc:128 * (fc + 1)],
                            IN["L1_attn_wq"].ap()[128 * fc:128 * (fc + 1),
                                                  128 * hp:128 * (hp + 1)]))
                        sG.dinc(GP.dma_start(
                            wk1[:, 128 * fc:128 * (fc + 1)],
                            IN["L1_attn_wk"].ap()[128 * fc:128 * (fc + 1),
                                                  128 * hp:128 * (hp + 1)]))
                        mGw = sG.dinc(GP.dma_start(
                            wv1[:, 128 * fc:128 * (fc + 1)],
                            IN["L1_attn_wv"].ap()[128 * fc:128 * (fc + 1),
                                                  128 * hp:128 * (hp + 1)]))
                    TE.wait_ge(sG.h, mGw)
                    SY.wait_ge(sD.h, sD.n)
                    for i in range(RQ):
                        u = uP[0]
                        uP[0] += 1
                        pq = pp3[u % 2]
                        if prm.get(u - 2) is not None:
                            TE.wait_ge(sV.h, prm[u - 2])
                        for fc in range(8):
                            TE.matmul(pq[:, 0:128],
                                      wq1[:, 128 * fc:128 * (fc + 1)],
                                      l1afm[fc][:, 128 * i:128 * (i + 1)],
                                      start=(fc == 0), stop=(fc == 7))
                        for fc in range(8):
                            TE.matmul(pq[:, 128:256],
                                      wk1[:, 128 * fc:128 * (fc + 1)],
                                      l1afm[fc][:, 128 * i:128 * (i + 1)],
                                      start=(fc == 0), stop=(fc == 7))
                        for fc in range(8):
                            ins = TE.matmul(pq[:, 256:384],
                                            l1afm[fc][:, 128 * i:128 * (i + 1)],
                                            wv1[:, 128 * fc:128 * (fc + 1)],
                                            start=(fc == 0), stop=False)
                        mT = sT.inc(TE.matmul(
                            pq[:, 256:384], ones_r[:],
                            bv1r[0:1, 128 * hp:128 * (hp + 1)],
                            start=False, stop=True))
                        VE.wait_ge(sT.h, mT)
                        VE.tensor_scalar(q1f[:, 128 * i:128 * (i + 1)],
                                         pq[:, 0:128], bqc[:, hp:hp + 1],
                                         None, ALU.add)
                        VE.tensor_scalar(k1f[:, 128 * i:128 * (i + 1)],
                                         pq[:, 128:256], bkc[:, hp:hp + 1],
                                         None, ALU.add)
                        prm[u] = sV.inc(VE.tensor_copy(
                            v1t[:, 128 * i:128 * (i + 1)], pq[:, 256:384]))
                    nc.all_engine_barrier()
                    for j in range(NCHUNK):
                        for h01 in range(2):
                            hb = 64 * h01
                            for kt in range(4):
                                uu = uP[0]
                                uP[0] += 1
                                if expm3.get(uu - 8) is not None:
                                    TE.wait_ge(sA.h, expm3[uu - 8])
                                mQ = sT.inc(TE.matmul(
                                    pk4[kt][:],
                                    k1f[hb:hb + 64,
                                        128 * (2 * j + kt):128 * (2 * j + kt + 1)],
                                    q1f[hb:hb + 64, 256 * j:256 * j + 512],
                                    start=True, stop=True))
                                AE.wait_ge(sT.h, mQ)
                                if avm3[0] is not None:
                                    AE.wait_ge(sT.h, avm3[0])
                                expm3[uu] = sA.inc(AE.activation(
                                    psb[kt][:], pk4[kt][:], AF.Exp,
                                    bias=l1eb[:, 2 * j + kt:2 * j + kt + 1],
                                    scale=0.125))
                            mlast = expm3[uP[0] - 1]
                            TE.wait_ge(sA.h, mlast)
                            if ctxm[0] is not None:
                                TE.wait_ge(sV.h, ctxm[0])
                            if lnm3[0] is not None:
                                TE.wait_ge(sA.h, lnm3[0])
                            for kt in range(4):
                                TE.matmul(avp[:],
                                          v1t[:, 128 * (2 * j + kt) + hb:
                                              128 * (2 * j + kt) + hb + 64],
                                          psb[kt][:],
                                          start=(kt == 0), stop=(kt == 3))
                            for kt in range(4):
                                ins = TE.matmul(zbp[:], ones64[:], psb[kt][:],
                                                start=(kt == 0), stop=(kt == 3))
                            avm3[0] = sT.inc(ins)
                            AE.wait_ge(sT.h, avm3[0])
                            af(AE.activation(z5[:], zbp[:], AF.Ln,
                                             bias=c_zeps[0:64, :]))
                            lnm3[0] = sA.inc(AE.activation(rz5[:], z5[:],
                                                           AF.Exp, scale=-1.0))
                            VE.wait_ge(sA.h, lnm3[0])
                            vf(VE.tensor_tensor(rz5[:], rz5[:],
                                                gamb[0:64, 512 * j:512 * (j + 1)],
                                                ALU.mult))
                            vf(VE.tensor_tensor(tm5[hb:hb + 64, :], avp[:],
                                                rz5[:], ALU.mult))
                            lo = max(384, 256 * j)
                            hi = min(2432, 256 * j + 512)
                            if lo < hi:
                                csl = ctx_t[:, 2048 * hp + lo - 384:
                                            2048 * hp + hi - 384][hb:hb + 64, :]
                                ctxm[0] = sV.inc(VE.tensor_tensor(
                                    csl, csl,
                                    tm5[hb:hb + 64, lo - 256 * j:hi - 256 * j],
                                    ALU.add))
                nc.all_engine_barrier()
        if "ctx" in taps:
            t = tap("ctx", [128, 8 * 2048], BF16)
            sD.dinc(SY.dma_start(t.ap(), ctx_t[:]))
            SY.wait_ge(sD.h, sD.n)
            nc.all_engine_barrier()
        esLn1.close()

        l2fm_t = es3.enter_context(nc.sbuf_tensor("l2fm", [128, 8 * 2048], BF16))
        l2fm = [l2fm_t[:, c * 2048:(c + 1) * 2048] for c in range(8)]
        with ExitStack() as esO3:
            wpj = esO3.enter_context(nc.sbuf_tensor("wpj", [128, 8 * D], BF16))
            bpt = esO3.enter_context(nc.sbuf_tensor("bpt", [128, D], F32))
            g2t = esO3.enter_context(nc.sbuf_tensor("g2t", [128, D], F32))
            b2t = esO3.enter_context(nc.sbuf_tensor("b2t", [128, D], F32))
            xnr = [esO3.enter_context(nc.sbuf_tensor(f"xnr{i}", [128, D], F32))
                   for i in range(2)]
            o2f = [esO3.enter_context(nc.sbuf_tensor(f"o2f{i}", [128, D], F32))
                   for i in range(2)]
            ln2b = esO3.enter_context(nc.sbuf_tensor("ln2bt", [128, D], BF16))
            pw3 = [esO3.enter_context(nc.psum_tensor(f"pw3{i}", [128, D], F32))
                   for i in range(2)]
            ptr4 = [esO3.enter_context(nc.psum_tensor(f"ptr4{i}", [128, 128], BF16))
                    for i in range(2)]
            for fc in range(8):
                sG.dinc(GP.dma_start(wpj[:, D * fc:D * (fc + 1)],
                                     IN["L1_attn_wproj"].ap()[128 * fc:128 * (fc + 1), :]))
            with ExitStack() as esT3:
                r1 = esT3.enter_context(nc.sbuf_tensor("r1b", [1, D], F32))
                r2 = esT3.enter_context(nc.sbuf_tensor("r2b", [1, D], F32))
                r3 = esT3.enter_context(nc.sbuf_tensor("r3b", [1, D], F32))
                sD.dinc(SY.dma_start(r1[:], IN["L1_attn_bproj"].ap()))
                sD.dinc(SY.dma_start(r2[:], IN["L1_attn_ln2_g"].ap()))
                sD.dinc(SY.dma_start(r3[:], IN["L1_attn_ln2_b"].ap()))
                SY.wait_ge(sD.h, sD.n)
                GP.wait_ge(sG.h, sG.n)
                nc.all_engine_barrier()
                for (row, dst) in ((r1, bpt), (r2, g2t), (r3, b2t)):
                    for half in range(2):
                        TE.matmul(pw3[0][:, 0:512], ones_r[:],
                                  row[0:1, 512 * half:512 * (half + 1)],
                                  start=True, stop=True)
                        nc.all_engine_barrier()
                        VE.tensor_copy(dst[:, 512 * half:512 * (half + 1)],
                                       pw3[0][:, 0:512])
                        nc.all_engine_barrier()
            evo3 = {}
            trm4 = {}
            for t_ in range(NOWN):
                pq = pw3[t_ % 2]
                if evo3.get(t_ - 2) is not None:
                    TE.wait_ge(sV.h, evo3[t_ - 2])
                for half in range(2):
                    for hp in range(8):
                        ins = TE.matmul(
                            pq[:, 512 * half:512 * (half + 1)],
                            ctx_t[:, 2048 * hp + 128 * t_:
                                  2048 * hp + 128 * (t_ + 1)],
                            wpj[:, D * hp + 512 * half:D * hp + 512 * (half + 1)],
                            start=(hp == 0), stop=(hp == 7))
                mT = sT.inc(ins)
                mDx = sD.dinc(SY.dma_start(
                    xnr[t_ % 2][:],
                    xn1_d.ap()[128 * (t_ + OWN0):128 * (t_ + OWN0 + 1), :]))
                VE.wait_ge(sT.h, mT)
                VE.wait_ge(sD.h, mDx)
                vf(VE.tensor_tensor(o2f[t_ % 2][:], pq[:], bpt[:], ALU.add))
                mV = sV.inc(VE.tensor_tensor(o2f[t_ % 2][:], o2f[t_ % 2][:],
                                             xnr[t_ % 2][:], ALU.add))
                SY.wait_ge(sV.h, mV)
                sD.dinc(SY.dma_start(out2_d.ap()[128 * t_:128 * (t_ + 1), :],
                                     o2f[t_ % 2][:]))
                VE.wait_ge(sV.h, mV)
                mL = ln_stats_apply(o2f[t_ % 2][:], ln2b[:], scr3[:], D,
                                    gb=g2t[:], bb=b2t[:])
                evo3[t_] = mL
                fm_transpose(ln2b, l2fm, 128 * t_, ptr4, trm4, mL)
        nc.all_engine_barrier()
        if "out2" in taps:
            t = tap("out2", [NOWN * 128, D], F32)
            sD.dinc(SY.dma_start(t.ap(), out2_d.ap()))
            SY.wait_ge(sD.h, sD.n)
            nc.all_engine_barrier()

        with ExitStack() as esM3:
            w1b3 = [esM3.enter_context(nc.sbuf_tensor(f"mw1b{i}", [128, 1024], BF16))
                    for i in range(2)]
            w2b3 = [esM3.enter_context(nc.sbuf_tensor(f"mw2b{i}", [128, 512], BF16))
                    for i in range(2)]
            hfm3 = esM3.enter_context(nc.sbuf_tensor("mhfm", [128, 32 * 768], BF16))
            b1s3 = esM3.enter_context(nc.sbuf_tensor("mb1s", [128, 32], F32))
            b2r3 = esM3.enter_context(nc.sbuf_tensor("mb2r", [1, D], F32))
            o2r = esM3.enter_context(nc.sbuf_tensor("o2r", [128, D], F32))
            s1f = esM3.enter_context(nc.sbuf_tensor("s1f", [128, D], F32))
            hp3 = esM3.enter_context(nc.psum_tensor("mhps", [128, 768], F32))
            pw63 = [esM3.enter_context(nc.psum_tensor(f"mpw{i}", [128, 512], F32))
                    for i in range(6)]
            sD.dinc(SY.dma_start(b1s3[:], IN["L1_attn_b1"].ap()))
            sD.dinc(SY.dma_start(b2r3[:], IN["L1_attn_b2"].ap()))
            SY.wait_ge(sD.h, sD.n)
            nc.all_engine_barrier()
            ttbs3 = [(0, 6), (6, 6), (12, 4)]
            prev_w1 = [None]
            prev_w2 = [None]
            prev_gel = [None]
            pwf = [None]
            for (q0, nq) in ttbs3:
                ncols = 128 * nq
                segs = [(0, min(512, ncols))]
                if ncols > 512:
                    segs.append((512, ncols - 512))
                w1m = {}
                gel3 = {}
                for hc in range(32):
                    if w1m.get(hc - 2) is not None:
                        SY.wait_ge(sT.h, w1m[hc - 2])
                    elif prev_w1[0] is not None:
                        SY.wait_ge(sT.h, prev_w1[0])
                    mW = sD.dinc(SY.dma_start(
                        w1b3[hc % 2][:],
                        bass.AP(wa1_bf, hc * 131072, [[1024, 128], [1, 1024]])))
                    TE.wait_ge(sD.h, mW)
                    if gel3.get(hc - 1) is not None:
                        TE.wait_ge(sA.h, gel3[hc - 1])
                    elif prev_gel[0] is not None:
                        TE.wait_ge(sA.h, prev_gel[0])
                    ins = None
                    for (a, w) in segs:
                        for fc in range(8):
                            ins = TE.matmul(
                                hp3[:, a:a + w],
                                w1b3[hc % 2][:, 128 * fc:128 * (fc + 1)],
                                l2fm[fc][:, 128 * q0 + a:128 * q0 + a + w],
                                start=(fc == 0), stop=(fc == 7))
                    w1m[hc] = sT.inc(ins)
                    AE.wait_ge(sT.h, w1m[hc])
                    gel3[hc] = sA.inc(AE.activation(
                        hfm3[:, 768 * hc:768 * hc + ncols], hp3[:, 0:ncols],
                        AF.Gelu, bias=b1s3[:, hc:hc + 1]))
                prev_w1[0] = w1m[31]
                prev_gel[0] = gel3[31]
                TE.wait_ge(sA.h, gel3[31])
                for odh in range(2):
                    w2m = {}
                    for hc in range(32):
                        if w2m.get(hc - 2) is not None:
                            SY.wait_ge(sT.h, w2m[hc - 2])
                        elif prev_w2[0] is not None:
                            SY.wait_ge(sT.h, prev_w2[0])
                        mW = sD.dinc(SY.dma_start(
                            w2b3[hc % 2][:],
                            bass.AP(wa2_bf, hc * 131072 + 512 * odh,
                                    [[1024, 128], [1, 512]])))
                        TE.wait_ge(sD.h, mW)
                        if hc == 0 and pwf[0] is not None:
                            TE.wait_ge(sV.h, pwf[0])
                        for t_ in range(nq):
                            ins = TE.matmul(
                                pw63[t_][:],
                                hfm3[:, 768 * hc + 128 * t_:
                                     768 * hc + 128 * (t_ + 1)],
                                w2b3[hc % 2][:],
                                start=(hc == 0), stop=False)
                        w2m[hc] = sT.inc(ins)
                    prev_w2[0] = w2m[31]
                    ins = None
                    for t_ in range(nq):
                        ins = TE.matmul(pw63[t_][:], ones_r[:],
                                        b2r3[0:1, 512 * odh:512 * (odh + 1)],
                                        start=False, stop=True)
                    mE = sT.inc(ins)
                    VE.wait_ge(sT.h, mE)
                    for t_ in range(nq):
                        mDo = sD.dinc(SY.dma_start(
                            o2r[:], out2_d.ap()[128 * (q0 + t_):
                                                128 * (q0 + t_ + 1), :]))
                        VE.wait_ge(sD.h, mDo)
                        mV2 = sV.inc(VE.tensor_tensor(
                            s1f[:, 0:512], pw63[t_][:],
                            o2r[:, 512 * odh:512 * (odh + 1)], ALU.add))
                        SY.wait_ge(sV.h, mV2)
                        sD.dinc(SY.dma_start(
                            bass.AP(s1_d, 128 * (q0 + t_) * D + 512 * odh,
                                    [[D, 128], [1, 512]]),
                            s1f[:, 0:512]))
                    pwf[0] = sV.n
        nc.all_engine_barrier()
        if "s1" in taps:
            t = tap("s1", [NOWN * 128, D], F32)
            sD.dinc(SY.dma_start(t.ap(), s1_d.ap()))
            SY.wait_ge(sD.h, sD.n)
            nc.all_engine_barrier()

        estar1 = es3.enter_context(nc.sbuf_tensor("estar1", [128, NOWN], F32))
        with ExitStack() as esG3:
            gwT = esG3.enter_context(nc.sbuf_tensor("gwT", [128, 64], F32))
            gbe = esG3.enter_context(nc.sbuf_tensor("gbe", [1, 8], F32))
            with ExitStack() as esT4:
                g8r = esT4.enter_context(nc.sbuf_tensor("g8r", [8, D], F32))
                eb8 = esT4.enter_context(nc.sbuf_tensor("eb8", [1, 8], F32))
                gb8 = esT4.enter_context(nc.sbuf_tensor("gb8", [1, 8], F32))
                pt8 = esT4.enter_context(nc.psum_tensor("pt8", [128, 8], F32))
                sD.dinc(SY.dma_start(g8r[:], IN["L1_ffn_gate_w"].ap()))
                sD.dinc(SY.dma_start(gb8[:], IN["L1_ffn_gate_b"].ap()))
                sD.dinc(SY.dma_start(eb8[:], IN["L1_ffn_expert_bias"].ap()))
                SY.wait_ge(sD.h, sD.n)
                nc.all_engine_barrier()
                VE.tensor_tensor(gbe[:], gb8[:], eb8[:], ALU.add)
                for fc in range(8):
                    nc.all_engine_barrier()
                    TE.transpose(pt8[:], g8r[:, 128 * fc:128 * (fc + 1)],
                                 id16[0:8, 0:8])
                    nc.all_engine_barrier()
                    VE.tensor_copy(gwT[:, 8 * fc:8 * (fc + 1)], pt8[:])
                nc.all_engine_barrier()
            xg3 = [esG3.enter_context(nc.sbuf_tensor(f"xg3{i}", [128, D], F32))
                   for i in range(2)]
            xT3 = [esG3.enter_context(nc.sbuf_tensor(f"xT3{i}", [128, 128], F32))
                   for i in range(2)]
            ptf3 = [esG3.enter_context(nc.psum_tensor(f"ptf3{i}", [128, 128], F32))
                    for i in range(2)]
            pg3 = esG3.enter_context(nc.psum_tensor("pg3", [128, 8], F32))
            lg3 = esG3.enter_context(nc.sbuf_tensor("lg3", [128, 8], F32))
            eq3 = esG3.enter_context(nc.sbuf_tensor("eq3", [128, 8], F32))
            t83 = esG3.enter_context(nc.sbuf_tensor("t83", [128, 8], F32))
            xgm = {}
            tev3 = {}
            mm3 = {}
            for t_ in range(NOWN):
                if xgm.get(t_ - 2) is not None:
                    SY.wait_ge(sT.h, xgm[t_ - 2])
                mD = sD.dinc(SY.dma_start(
                    xg3[t_ % 2][:],
                    xn1_d.ap()[128 * (t_ + OWN0):128 * (t_ + OWN0 + 1), :]))
                for fc in range(8):
                    u = t_ * 8 + fc
                    TE.wait_ge(sD.h, mD)
                    if tev3.get(u - 2) is not None:
                        TE.wait_ge(sV.h, tev3[u - 2])
                    mTr = sT.inc(TE.transpose(
                        ptf3[fc % 2][:], xg3[t_ % 2][:, 128 * fc:128 * (fc + 1)],
                        id_f[:]))
                    VE.wait_ge(sT.h, mTr)
                    if mm3.get(u - 2) is not None:
                        VE.wait_ge(sT.h, mm3[u - 2])
                    mEv = sV.inc(VE.tensor_copy(xT3[fc % 2][:], ptf3[fc % 2][:]))
                    TE.wait_ge(sV.h, mEv)
                    mm3[u] = sT.inc(TE.matmul(pg3[:], xT3[fc % 2][:],
                                              gwT[:, 8 * fc:8 * (fc + 1)],
                                              start=(fc == 0), stop=False))
                    tev3[u] = mEv
                xgm[t_] = sT.inc(TE.matmul(pg3[:], ones_r[:], gbe[:],
                                           start=False, stop=True))
                VE.wait_ge(sT.h, xgm[t_])
                vf(VE.tensor_copy(lg3[:], pg3[:]))
                vf(VE.reduce_max(col8[:, 0:1], lg3[:], axis=AX.X))
                vf(VE.tensor_scalar(eq3[:], lg3[:], col8[:, 0:1], None,
                                    ALU.is_equal))
                vf(VE.tensor_tensor(t83[:], eq3[:], iota8[:], ALU.mult))
                sV.inc(VE.reduce_max(estar1[:, t_:t_ + 1], t83[:], axis=AX.X))
        nc.all_engine_barrier()
        g1_acc = es3.enter_context(nc.sbuf_tensor("g1_acc", [128, NOWN * D], BF16))
        emit_moe("L1", NOWN, xn1fm, 0, estar1, g1_acc, w1l1_bf, w2l1_bf,
                 IN["L1_ffn_b1"], IN["L1_ffn_b2"], AF.Relu)
        nc.all_engine_barrier()
        if "estar1" in taps:
            t = tap("estar1", [128, NOWN], F32)
            sD.dinc(SY.dma_start(t.ap(), estar1[:]))
        if "g1acc" in taps:
            t = tap("g1acc", [128, NOWN * D], BF16)
            sD.dinc(SY.dma_start(t.ap(), g1_acc[:]))
        if taps:
            SY.wait_ge(sD.h, sD.n)
            nc.all_engine_barrier()

        with ExitStack() as esF:
            xc3 = [esF.enter_context(nc.sbuf_tensor(f"xc3{i}", [128, D], F32))
                   for i in range(2)]
            sc3 = [esF.enter_context(nc.sbuf_tensor(f"sc3{i}", [128, D], F32))
                   for i in range(2)]
            gs3 = [esF.enter_context(nc.sbuf_tensor(f"gs3{i}", [128, D], F32))
                   for i in range(2)]
            yo3 = [esF.enter_context(nc.sbuf_tensor(f"yo3{i}", [128, D], F32))
                   for i in range(2)]
            ym = {}
            mVs = {}
            for t_ in range(NOWN):
                if ym.get(t_ - 2) is not None:
                    SY.wait_ge(sD.h, ym[t_ - 2])
                mD1 = sD.dinc(SY.dma_start(
                    xc3[t_ % 2][:],
                    x1_d.ap()[128 * (t_ + OWN0):128 * (t_ + OWN0 + 1), :]))
                mD2 = sD.dinc(SY.dma_start(
                    sc3[t_ % 2][:], s1_d.ap()[128 * t_:128 * (t_ + 1), :]))
                if mVs.get(t_ - 2) is not None:
                    AE.wait_ge(sV.h, mVs[t_ - 2])
                mA = sA.inc(AE.activation(gs3[t_ % 2][:],
                                          g1_acc[:, D * t_:D * (t_ + 1)],
                                          AF.Silu))
                VE.wait_ge(sA.h, mA)
                VE.wait_ge(sD.h, max(mD1, mD2))
                vf(VE.tensor_tensor(gs3[t_ % 2][:], sc3[t_ % 2][:],
                                    gs3[t_ % 2][:], ALU.mult))
                mV = sV.inc(VE.tensor_tensor(yo3[t_ % 2][:], xc3[t_ % 2][:],
                                             gs3[t_ % 2][:], ALU.add))
                mVs[t_] = mV
                SY.wait_ge(sV.h, mV)
                ym[t_] = sD.dinc(SY.dma_start(
                    yout_d.ap()[128 * t_:128 * (t_ + 1), :], yo3[t_ % 2][:]))
        SY.wait_ge(sD.h, sD.n)
    nc.all_engine_barrier()
    return nc, tap_outs



def _gamma_host(h):
    S = 4096
    w2, w4 = 256, 128
    coef = {}
    rows_set = np.full(S, -1, np.int64)
    alpha = np.zeros(S, np.float64)
    res = {i: np.zeros(S, np.float64) for i in range(0, S, w2)}
    cur = np.zeros((S,), np.float64)
    owner = {}
    combo = np.zeros((S, len(res)), np.float64)
    idx = {i: k for k, i in enumerate(sorted(res))}
    for i in range(0, S, w2):
        s0, e0 = max(0, i - w4), min(S, i + 512 - w4)
        k = idx[i]
        if i == 0:
            combo[s0:e0, :] = 0.0
            combo[s0:e0, k] = 1.0
        else:
            oe = min(i + w4, S)
            n = oe - s0
            al = np.linspace(0.0, 1.0, n)
            combo[s0:oe, :] *= (1.0 - al)[:, None]
            combo[s0:oe, k] += al
            combo[oe:e0, :] = 0.0
            combo[oe:e0, k] = 1.0
    gam = np.zeros((NCHUNK, 512), np.float32)
    own_lo, own_hi = 2048 * h, 2048 * h + 2048
    for j in range(NCHUNK):
        i_j = 256 * j + 2048 * h - 256
        if i_j not in idx:
            continue
        k = idx[i_j]
        for c in range(512):
            a = 256 * j + (2048 * h - 384) + c
            if own_lo <= a < own_hi:
                gam[j, c] = combo[a, k]
    return gam


def make_core_inputs(I, c):
    b, h = c // 2, c % 2
    x = np.asarray(I["L0_attn_x"][b], np.float32)
    xstat = np.zeros((SS_ROWS, D), np.float32)
    xstat[1024:1024 + 4096] = x
    lo = 2048 * h - 896
    xin = np.zeros((RX_ROWS, D), np.float32)
    a0, a1 = max(0, lo), min(4096, lo + RX_ROWS)
    xin[a0 - lo:a1 - lo] = x[a0:a1]
    l0eb = np.zeros((128, RK), np.float32)
    for jj in range(RK):
        abs_lo = 128 * jj - 512 + (2048 * h - 384)
        bad = (abs_lo + 128 <= 0) or (abs_lo >= 4096)
        l0eb[:, jj] = NEG if bad else L0_EXPB
    l1eb = np.zeros((128, RQ), np.float32)
    for jj in range(RQ):
        abs_lo = 128 * jj + (2048 * h - 384)
        bad = (abs_lo + 128 <= 0) or (abs_lo >= 4096)
        l1eb[:, jj] = NEG if bad else L1_EXPB
    gam = _gamma_host(h).reshape(1, NCHUNK * 512)

    m = {
        "xstat": xstat, "xin": xin, "l0eb": l0eb, "l1eb": l1eb, "gam": gam,
        "dynoff": np.array([[1 + 16 * h, 0]], np.int32),
        "l1bqc": np.ascontiguousarray(
            np.asarray(I["L1_attn_bq"], np.float32).reshape(8, 128).T),
        "l1bkc": np.ascontiguousarray(
            np.asarray(I["L1_attn_bk"], np.float32).reshape(8, 128).T),
    }
    for n in ("L0_attn_w_q", "L0_attn_w_k", "L0_attn_w_v", "L0_attn_w_o",
              "L0_ffn_gate_w", "L1_attn_wq", "L1_attn_wk", "L1_attn_wv",
              "L1_attn_wproj", "L1_attn_w1", "L1_attn_w2", "L1_ffn_w1",
              "L1_ffn_w2", "L0_ffn_w1", "L0_ffn_w2", "L1_ffn_gate_w"):
        m[n] = np.ascontiguousarray(np.asarray(I[n], np.float32))
    for n in ("L0_attn_b_q", "L0_attn_b_k", "L0_attn_b_v", "L0_attn_b_o",
              "L0_ffn_gate_b", "L0_attn_rms_q_w", "L0_attn_rms_k_w",
              "L1_attn_ln1_g", "L1_attn_ln1_b", "L1_attn_bq", "L1_attn_bk",
              "L1_attn_bv", "L1_attn_bproj", "L1_attn_ln2_g", "L1_attn_ln2_b",
              "L1_attn_b2", "L1_ffn_gate_b", "L1_ffn_expert_bias"):
        m[n] = np.asarray(I[n], np.float32).reshape(1, -1)
    m["L0_ffn_b1"] = np.ascontiguousarray(
        np.asarray(I["L0_ffn_b1"], np.float32).reshape(8, 32, 128)
        .transpose(0, 2, 1))
    m["L1_ffn_b1"] = np.ascontiguousarray(
        np.asarray(I["L1_ffn_b1"], np.float32).reshape(8, 32, 128)
        .transpose(0, 2, 1))
    m["L1_attn_b1"] = np.ascontiguousarray(
        np.asarray(I["L1_attn_b1"], np.float32).reshape(32, 128).T)
    m["L0_ffn_b2"] = np.asarray(I["L0_ffn_b2"], np.float32).reshape(8, 1, D)
    m["L1_ffn_b2"] = np.asarray(I["L1_ffn_b2"], np.float32).reshape(8, 1, D)
    return m


_BUILT = {}


def kernel(**inputs):
    key = "main"
    if key not in _BUILT:
        _BUILT[key] = build()
    nc, _ = _BUILT[key]
    in_maps = [make_core_inputs(inputs, c) for c in range(8)]
    res = run_bass_kernel_spmd(nc, in_maps, core_ids=list(range(8)))
    out = np.empty((4, 4096, D), np.float32)
    for c in range(8):
        out[c // 2, (c % 2) * 2048:(c % 2) * 2048 + 2048] = res.results[c]["yout"]
    return out
